# version 02 (final)
# baseline (speedup 1.0000x reference)
import numpy as np
import ml_dtypes
from contextlib import ExitStack
import concourse.bass as bass
import concourse.mybir as mybir
from concourse.bass_utils import run_bass_kernel_spmd

F32 = mybir.dt.float32
BF16 = mybir.dt.bfloat16
I32 = mybir.dt.int32
AF = mybir.ActivationFunctionType
OP = mybir.AluOpType
AX = mybir.AxisListType
BF = ml_dtypes.bfloat16

B, S, D = 2, 16384, 2048
T = B * S
NC = 8
TPC = T // NC
NH, DH = 16, 128
M0 = 16384
IN1 = 5504
RS = 1.0 / float(np.sqrt(np.float32(D)))
EPS = 1e-5
NVLUT = [0.0, 0.5, 1.0, 1.5, 2.0, 3.0, 4.0, 6.0,
         0.0, -0.5, -1.0, -1.5, -2.0, -3.0, -4.0, -6.0]
C1 = 0.044715 * 0.7978845608028654
C2 = 0.7978845608028654


class Prog:

    def __init__(self):
        self.items = []

    def op(self, eng, fn, inc=1):
        self.items.append((eng, fn, inc))

    def dma(self, fn):
        self.op("sync", fn, 16)

    def v(self, fn):
        self.op("vector", fn, 1)

    def s(self, fn):
        self.op("scalar", fn, 1)

    def t(self, fn):
        self.op("tensor", fn, 1)

    def emit(self, nc, sem, block):
        pre = []
        tot = 0
        for (_, _, inc) in self.items:
            pre.append(tot)
            tot += inc
        total = tot

        def mk(engname):
            def body(eng):
                for (en, fn, inc), w in zip(self.items, pre):
                    if en != engname:
                        continue
                    if w > 0:
                        eng.wait_ge(sem, w)
                    instr = fn(eng)
                    instr.then_inc(sem, inc)
                if engname == "sync":
                    eng.wait_ge(sem, total)
            return body

        engs = {en for (en, _, _) in self.items}
        for en in ["sync", "vector", "scalar", "tensor", "gpsimd"]:
            if en in engs or en == "sync":
                getattr(block, en)(mk(en))




def MM(p, out, lhsT, rhs, start, stop, skip=False):
    N = rhs.shape[-1]
    for off in range(0, N, 512):
        w = min(512, N - off)
        p.t(lambda e, off=off, w=w: e.matmul(
            out=out[:, off:off + w], lhsT=lhsT, rhs=rhs[:, off:off + w],
            start=start, stop=stop, skip_group_check=skip))

def rearr_pj(ap, p=128):
    return ap.rearrange("(j p) t -> p j t", p=p)



def build_dequant():
    nc = bass.Bass("TRN2", target_bir_lowering=False)
    gW = nc.dram_tensor("gW", [2048, 1024], I32, kind="ExternalInput").ap()
    gS = nc.dram_tensor("gS", [2048, 64], F32, kind="ExternalInput").ap()
    uW = nc.dram_tensor("uW", [2048, 1024], I32, kind="ExternalInput").ap()
    uS = nc.dram_tensor("uS", [2048, 64], F32, kind="ExternalInput").ap()
    dW = nc.dram_tensor("dW", [2048, 1024], I32, kind="ExternalInput").ap()
    dS = nc.dram_tensor("dS", [2048, 64], F32, kind="ExternalInput").ap()
    ggI = nc.dram_tensor("ggI", [688, 2048], I32, kind="ExternalInput").ap()
    guI = nc.dram_tensor("guI", [688, 2048], I32, kind="ExternalInput").ap()
    gdI = nc.dram_tensor("gdI", [2048, 688], I32, kind="ExternalInput").ap()
    lutb = nc.dram_tensor("lutb", [128, 16], F32, kind="ExternalInput").ap()
    idn = nc.dram_tensor("idn", [128, 128], BF16, kind="ExternalInput").ap()

    wgt = nc.dram_tensor("wgt", [2048, 2048], BF16, kind="ExternalOutput").ap()
    wut = nc.dram_tensor("wut", [2048, 2048], BF16, kind="ExternalOutput").ap()
    wdt = nc.dram_tensor("wdt", [2048, 2048], BF16, kind="ExternalOutput").ap()
    ggt = nc.dram_tensor("ggt", [2048, 688], BF16, kind="ExternalOutput").ap()
    gut = nc.dram_tensor("gut", [2048, 688], BF16, kind="ExternalOutput").ap()
    gdt = nc.dram_tensor("gdt", [688, 2048], BF16, kind="ExternalOutput").ap()

    p = Prog()
    es = ExitStack()
    with es:
        wi = es.enter_context(nc.sbuf_tensor("wi", [128, 2048], I32))
        wn = es.enter_context(nc.sbuf_tensor("wn", [128, 2048], I32))
        wf = es.enter_context(nc.sbuf_tensor("wf", [128, 2048], F32))
        tmp = es.enter_context(nc.sbuf_tensor("tmp", [128, 2048], F32))
        vals = es.enter_context(nc.sbuf_tensor("vals", [128, 2048], F32))
        sct = es.enter_context(nc.sbuf_tensor("sct", [128, 64], F32))
        vb = es.enter_context(nc.sbuf_tensor("vb", [128, 2048], BF16))
        lt = es.enter_context(nc.sbuf_tensor("lt", [128, 16], F32))
        idt = es.enter_context(nc.sbuf_tensor("idt", [128, 128], BF16))
        ps = es.enter_context(nc.psum_tensor("ps", [128, 128], BF16))
        tb = es.enter_context(nc.sbuf_tensor("tb", [128, 128], BF16))
        sem = es.enter_context(nc.semaphore("sem"))
        block = es.enter_context(nc.Block())
        p.dma(lambda e: e.dma_start(out=lt[:], in_=lutb))
        p.dma(lambda e: e.dma_start(out=idt[:], in_=idn))

        def lut_const(dst, idxf, pw):
            p.v(lambda e: e.memset(dst[0:pw, :], 0.0))
            for j in range(16):
                if NVLUT[j] == 0.0:
                    continue
                p.v(lambda e, j=j: e.tensor_scalar(
                    out=tmp[0:pw, :], in0=idxf[0:pw, :], scalar1=float(j),
                    scalar2=NVLUT[j], op0=OP.is_equal, op1=OP.mult))
                p.v(lambda e: e.tensor_tensor(
                    out=dst[0:pw, :], in0=dst[0:pw, :], in1=tmp[0:pw, :], op=OP.add))

        def deq_nv(src, scales, r):
            p.dma(lambda e: e.dma_start(out=wi[:, 0:1024], in_=src[r * 128:(r + 1) * 128, :]))
            p.dma(lambda e: e.dma_start(out=sct[:], in_=scales[r * 128:(r + 1) * 128, :]))
            v2 = vals.rearrange("p (k two) -> p k two", two=2)
            p.v(lambda e: e.tensor_scalar(out=wn[:, 0:1024], in0=wi[:, 0:1024],
                                          scalar1=15, scalar2=None, op0=OP.bitwise_and))
            p.v(lambda e: e.tensor_copy(out=wf[:, 0:1024], in_=wn[:, 0:1024]))
            p.v(lambda e: e.memset(vals[:], 0.0))
            for j in range(16):
                if NVLUT[j] == 0.0:
                    continue
                p.v(lambda e, j=j: e.tensor_scalar(
                    out=tmp[:, 0:1024], in0=wf[:, 0:1024], scalar1=float(j),
                    scalar2=NVLUT[j], op0=OP.is_equal, op1=OP.mult))
                p.v(lambda e: e.tensor_tensor(
                    out=v2[:, :, 0:1], in0=v2[:, :, 0:1],
                    in1=tmp[:, 0:1024].rearrange("p (k one) -> p k one", one=1), op=OP.add))
            p.v(lambda e: e.tensor_scalar(out=wn[:, 0:1024], in0=wi[:, 0:1024],
                                          scalar1=4, scalar2=15,
                                          op0=OP.logical_shift_right, op1=OP.bitwise_and))
            p.v(lambda e: e.tensor_copy(out=wf[:, 0:1024], in_=wn[:, 0:1024]))
            for j in range(16):
                if NVLUT[j] == 0.0:
                    continue
                p.v(lambda e, j=j: e.tensor_scalar(
                    out=tmp[:, 0:1024], in0=wf[:, 0:1024], scalar1=float(j),
                    scalar2=NVLUT[j], op0=OP.is_equal, op1=OP.mult))
                p.v(lambda e: e.tensor_tensor(
                    out=v2[:, :, 1:2], in0=v2[:, :, 1:2],
                    in1=tmp[:, 0:1024].rearrange("p (k one) -> p k one", one=1), op=OP.add))
            v3 = vals.rearrange("p (k b) -> p k b", b=32)
            vb3 = vb.rearrange("p (k b) -> p k b", b=32)
            p.v(lambda e: e.tensor_tensor(
                out=vb3[:], in0=v3[:],
                in1=sct[:].unsqueeze(2).broadcast_to((128, 64, 32)), op=OP.mult))

        for (src, scales, dst) in [(gW, gS, wgt), (uW, uS, wut)]:
            for r in range(16):
                deq_nv(src, scales, r)
                for j in range(16):
                    p.t(lambda e, j=j: e.transpose(
                        out=ps[:], in_=vb[:, j * 128:(j + 1) * 128], identity=idt[:]))
                    p.v(lambda e: e.tensor_copy(out=tb[:], in_=ps[:]))
                    p.dma(lambda e, j=j, r=r, dst=dst: e.dma_start(
                        out=dst[j * 128:(j + 1) * 128, r * 128:(r + 1) * 128], in_=tb[:]))
        for r in range(16):
            deq_nv(dW, dS, r)
            for j in range(16):
                p.t(lambda e, j=j: e.transpose(
                    out=ps[:], in_=vb[:, j * 128:(j + 1) * 128], identity=idt[:]))
                p.v(lambda e: e.tensor_copy(out=tb[:], in_=ps[:]))
                p.dma(lambda e, j=j, r=r: e.dma_start(
                    out=wdt[j * 128:(j + 1) * 128, r * 128:(r + 1) * 128], in_=tb[:]))

        def deq_ghost(srcslice, pw, fw):
            p.dma(lambda e: e.dma_start(out=wi[0:pw, 0:fw], in_=srcslice))
            p.v(lambda e: e.tensor_copy(out=wf[0:pw, 0:fw], in_=wi[0:pw, 0:fw]))
            p.v(lambda e: e.memset(vals[0:pw, 0:fw], 0.0))
            for j in range(16):
                p.v(lambda e, j=j: e.tensor_scalar(
                    out=tmp[0:pw, 0:fw], in0=wf[0:pw, 0:fw], scalar1=float(j),
                    scalar2=lt[0:pw, j:j + 1], op0=OP.is_equal, op1=OP.mult))
                p.v(lambda e: e.tensor_tensor(
                    out=vals[0:pw, 0:fw], in0=vals[0:pw, 0:fw], in1=tmp[0:pw, 0:fw], op=OP.add))
            p.v(lambda e: e.tensor_copy(out=vb[0:pw, 0:fw], in_=vals[0:pw, 0:fw]))

        rows = [(i * 128, min(128, 688 - i * 128)) for i in range(6)]
        for (src, dst) in [(ggI, ggt), (guI, gut)]:
            for (r0, pw) in rows:
                deq_ghost(src[r0:r0 + pw, :], pw, 2048)
                for j in range(16):
                    p.t(lambda e, j=j, pw=pw: e.transpose(
                        out=ps[0:128, 0:pw], in_=vb[0:pw, j * 128:(j + 1) * 128],
                        identity=idt[0:pw, 0:pw]))
                    p.v(lambda e, pw=pw: e.tensor_copy(out=tb[0:128, 0:pw], in_=ps[0:128, 0:pw]))
                    p.dma(lambda e, j=j, r0=r0, pw=pw, dst=dst: e.dma_start(
                        out=dst[j * 128:(j + 1) * 128, r0:r0 + pw], in_=tb[0:128, 0:pw]))
        for r in range(16):
            deq_ghost(gdI[r * 128:(r + 1) * 128, :], 128, 688)
            for (c0, cw) in rows:
                p.t(lambda e, c0=c0, cw=cw: e.transpose(
                    out=ps[0:cw, 0:128], in_=vb[:, c0:c0 + cw], identity=idt[:]))
                p.v(lambda e, cw=cw: e.tensor_copy(out=tb[0:cw, 0:128], in_=ps[0:cw, 0:128]))
                p.dma(lambda e, c0=c0, cw=cw, r=r: e.dma_start(
                    out=gdt[c0:c0 + cw, r * 128:(r + 1) * 128], in_=tb[0:cw, 0:128]))

        p.emit(nc, sem, block)
    return nc



def build_qkv():
    nc = bass.Bass("TRN2", target_bir_lowering=False)
    x = nc.dram_tensor("x", [TPC, D], F32, kind="ExternalInput").ap()
    wq = nc.dram_tensor("wq", [3 * D, D], F32, kind="ExternalInput").ap()
    bq = nc.dram_tensor("bq", [3 * D, 1], F32, kind="ExternalInput").ap()
    idn = nc.dram_tensor("idn", [128, 128], BF16, kind="ExternalInput").ap()
    xnt = nc.dram_tensor("xnt", [D, TPC], BF16, kind="ExternalOutput").ap()
    qt = nc.dram_tensor("qt", [D, TPC], BF16, kind="ExternalOutput").ap()
    kt = nc.dram_tensor("kt", [D, TPC], BF16, kind="ExternalOutput").ap()
    vo = nc.dram_tensor("vo", [TPC, D], BF16, kind="ExternalOutput").ap()

    p = Prog()
    es = ExitStack()
    with es:
        xt = es.enter_context(nc.sbuf_tensor("xt", [128, D], F32))
        xc = es.enter_context(nc.sbuf_tensor("xc", [128, D], F32))
        sq = es.enter_context(nc.sbuf_tensor("sq", [128, D], F32))
        st = es.enter_context(nc.sbuf_tensor("st", [128, 8], F32))
        xnb = es.enter_context(nc.sbuf_tensor("xnb", [128, D], BF16))
        xnT = es.enter_context(nc.sbuf_tensor("xnT", [128, 16, TPC], BF16))
        wtf = es.enter_context(nc.sbuf_tensor("wtf", [128, D], F32))
        wtb = es.enter_context(nc.sbuf_tensor("wtb", [128, D], BF16))
        wT = es.enter_context(nc.sbuf_tensor("wT", [128, 16, 128], BF16))
        bt = es.enter_context(nc.sbuf_tensor("bt", [128, 1], F32))
        ob = es.enter_context(nc.sbuf_tensor("ob", [128, 1024], BF16))
        vtb = es.enter_context(nc.sbuf_tensor("vtb", [128, 128], BF16))
        idt = es.enter_context(nc.sbuf_tensor("idt", [128, 128], BF16))
        psb = es.enter_context(nc.psum_tensor("psb", [128, 128], BF16))
        mm = es.enter_context(nc.psum_tensor("mm", [128, 1024], F32))
        sem = es.enter_context(nc.semaphore("sem"))
        block = es.enter_context(nc.Block())
        p.dma(lambda e: e.dma_start(out=idt[:], in_=idn))
        p.v(lambda e: e.memset(st[:, 4:5], EPS))
        for tt in range(TPC // 128):
            p.dma(lambda e, tt=tt: e.dma_start(out=xt[:], in_=x[tt * 128:(tt + 1) * 128, :]))
            p.v(lambda e: e.tensor_reduce(out=st[:, 0:1], in_=xt[:], axis=AX.X, op=OP.add))
            p.v(lambda e: e.tensor_scalar(out=st[:, 0:1], in0=st[:, 0:1],
                                          scalar1=1.0 / D, scalar2=None, op0=OP.mult))
            p.v(lambda e: e.tensor_scalar(out=xc[:], in0=xt[:], scalar1=st[:, 0:1],
                                          scalar2=None, op0=OP.subtract))
            p.s(lambda e: e.activation(out=sq[:], in_=xc[:], func=AF.Square))
            p.v(lambda e: e.tensor_reduce(out=st[:, 1:2], in_=sq[:], axis=AX.X, op=OP.add))
            p.s(lambda e: e.activation(out=st[:, 2:3], in_=st[:, 1:2], func=AF.Sqrt,
                                       scale=1.0 / D, bias=st[:, 4:5]))
            p.v(lambda e: e.reciprocal(out=st[:, 3:4], in_=st[:, 2:3]))
            p.v(lambda e: e.tensor_scalar(out=xnb[:], in0=xc[:], scalar1=st[:, 3:4],
                                          scalar2=None, op0=OP.mult))
            for j in range(16):
                p.t(lambda e, j=j: e.transpose(out=psb[:], in_=xnb[:, j * 128:(j + 1) * 128],
                                               identity=idt[:]))
                p.v(lambda e, j=j, tt=tt: e.tensor_copy(
                    out=xnT[:, j, tt * 128:(tt + 1) * 128], in_=psb[:]))
        for j in range(16):
            p.dma(lambda e, j=j: e.dma_start(out=xnt[j * 128:(j + 1) * 128, :], in_=xnT[:, j, :]))
        for m in range(48):
            p.dma(lambda e, m=m: e.dma_start(out=wtf[:], in_=wq[m * 128:(m + 1) * 128, :]))
            p.dma(lambda e, m=m: e.dma_start(out=bt[:], in_=bq[m * 128:(m + 1) * 128, :]))
            p.s(lambda e: e.activation(out=wtb[:], in_=wtf[:], func=AF.Copy))
            for j in range(16):
                p.t(lambda e, j=j: e.transpose(out=psb[:], in_=wtb[:, j * 128:(j + 1) * 128],
                                               identity=idt[:]))
                p.v(lambda e, j=j: e.tensor_copy(out=wT[:, j, :], in_=psb[:]))
            for tc in range(TPC // 1024):
                for j in range(16):
                    MM(p, mm[:], wT[:, j, :], xnT[:, j, tc * 1024:(tc + 1) * 1024],
                       start=(j == 0), stop=(j == 15))
                p.v(lambda e: e.tensor_scalar(out=ob[:], in0=mm[:], scalar1=bt[:, 0:1],
                                              scalar2=None, op0=OP.add))
                if m < 32:
                    dst = qt if m < 16 else kt
                    mm_ = m if m < 16 else m - 16
                    p.dma(lambda e, dst=dst, mm_=mm_, tc=tc: e.dma_start(
                        out=dst[mm_ * 128:(mm_ + 1) * 128, tc * 1024:(tc + 1) * 1024], in_=ob[:]))
                else:
                    for sblk in range(8):
                        p.t(lambda e, sblk=sblk: e.transpose(
                            out=psb[:], in_=ob[:, sblk * 128:(sblk + 1) * 128], identity=idt[:]))
                        p.v(lambda e: e.tensor_copy(out=vtb[:], in_=psb[:]))
                        p.dma(lambda e, m=m, tc=tc, sblk=sblk: e.dma_start(
                            out=vo[tc * 1024 + sblk * 128: tc * 1024 + (sblk + 1) * 128,
                                   (m - 32) * 128:(m - 31) * 128], in_=vtb[:]))
        p.emit(nc, sem, block)
    return nc



def build_attn0():
    nc = bass.Bass("TRN2", target_bir_lowering=False)
    qts = nc.dram_tensor("qts", [D, TPC], BF16, kind="ExternalInput").ap()
    ktb = nc.dram_tensor("ktb", [D, S], BF16, kind="ExternalInput").ap()
    vb = nc.dram_tensor("vb", [S, D], BF16, kind="ExternalInput").ap()
    wo = nc.dram_tensor("wo", [D, D], F32, kind="ExternalInput").ap()
    idn = nc.dram_tensor("idn", [128, 128], BF16, kind="ExternalInput").ap()
    idf = nc.dram_tensor("idf", [128, 128], F32, kind="ExternalInput").ap()
    ao = nc.dram_tensor("ao", [TPC, D], F32, kind="ExternalOutput").ap()

    p = Prog()
    es = ExitStack()
    with es:
        woT = es.enter_context(nc.sbuf_tensor("woT", [128, 16, D], BF16))
        wtf = es.enter_context(nc.sbuf_tensor("wtf", [128, 1024], F32))
        wtb = es.enter_context(nc.sbuf_tensor("wtb", [128, 1024], BF16))
        qtile = es.enter_context(nc.sbuf_tensor("qtile", [128, 16, 128], BF16))
        ktile = es.enter_context(nc.sbuf_tensor("ktile", [128, 16, 1024], BF16))
        vt2 = es.enter_context(nc.sbuf_tensor("vt2", [128, 2, D], BF16))
        scores = es.enter_context(nc.sbuf_tensor("scores", [128, S], F32))
        pT = es.enter_context(nc.sbuf_tensor("pT", [128, 128], BF16))
        ctxb = es.enter_context(nc.sbuf_tensor("ctxb", [128, D], BF16))
        cT = es.enter_context(nc.sbuf_tensor("cT", [128, 16, 128], BF16))
        yf = es.enter_context(nc.sbuf_tensor("yf", [128, 1024], F32))
        st = es.enter_context(nc.sbuf_tensor("st", [128, 4], F32))
        idt = es.enter_context(nc.sbuf_tensor("idt", [128, 128], BF16))
        idtf = es.enter_context(nc.sbuf_tensor("idtf", [128, 128], F32))
        psb = es.enter_context(nc.psum_tensor("psb", [128, 128], BF16))
        psf = es.enter_context(nc.psum_tensor("psf", [128, 128], F32))
        tb = es.enter_context(nc.sbuf_tensor("tb", [128, 128], BF16))
        tf = es.enter_context(nc.sbuf_tensor("tf", [128, 128], F32))
        scp = es.enter_context(nc.psum_tensor("scp", [128, 1024], F32))
        ctx = es.enter_context(nc.psum_tensor("ctx", [128, D], F32))
        sem = es.enter_context(nc.semaphore("sem"))
        block = es.enter_context(nc.Block())
        p.dma(lambda e: e.dma_start(out=idt[:], in_=idn))
        p.dma(lambda e: e.dma_start(out=idtf[:], in_=idf))
        for r in range(16):
            for hf in range(2):
                p.dma(lambda e, r=r, hf=hf: e.dma_start(
                    out=wtf[:], in_=wo[r * 128:(r + 1) * 128, hf * 1024:(hf + 1) * 1024]))
                p.s(lambda e: e.activation(out=wtb[:], in_=wtf[:], func=AF.Copy))
                for j in range(8):
                    jj = hf * 8 + j
                    p.t(lambda e, j=j: e.transpose(
                        out=psb[:], in_=wtb[:, j * 128:(j + 1) * 128], identity=idt[:]))
                    p.v(lambda e, jj=jj, r=r: e.tensor_copy(
                        out=woT[:, jj, r * 128:(r + 1) * 128], in_=psb[:]))
        for tt in range(TPC // 128):
            p.dma(lambda e, tt=tt: e.dma_start(
                out=qtile[:], in_=rearr_pj(qts[:, tt * 128:(tt + 1) * 128])))
            for kc in range(S // 1024):
                p.dma(lambda e, kc=kc: e.dma_start(
                    out=ktile[:], in_=rearr_pj(ktb[:, kc * 1024:(kc + 1) * 1024])))
                for j in range(16):
                    MM(p, scp[:], qtile[:, j, :], ktile[:, j, :],
                       start=(j == 0), stop=(j == 15))
                p.v(lambda e, kc=kc: e.tensor_scalar(
                    out=scores[:, kc * 1024:(kc + 1) * 1024], in0=scp[:],
                    scalar1=RS, scalar2=None, op0=OP.mult))
            p.v(lambda e: e.tensor_reduce(out=st[:, 0:1], in_=scores[:], axis=AX.X, op=OP.max))
            p.v(lambda e: e.tensor_scalar(out=st[:, 1:2], in0=st[:, 0:1], scalar1=-1.0,
                                          scalar2=None, op0=OP.mult))
            p.s(lambda e: e.activation(out=scores[:], in_=scores[:], func=AF.Exp,
                                       bias=st[:, 1:2]))
            p.v(lambda e: e.tensor_reduce(out=st[:, 2:3], in_=scores[:], axis=AX.X, op=OP.add))
            p.v(lambda e: e.reciprocal(out=st[:, 3:4], in_=st[:, 2:3]))
            for kg in range(S // 256):
                p.dma(lambda e, kg=kg: e.dma_start(
                    out=vt2[:], in_=vb[kg * 256:(kg + 1) * 256, :].rearrange(
                        "(g p) d -> p g d", p=128)))
                for g in range(2):
                    kb = kg * 2 + g
                    p.t(lambda e, kb=kb: e.transpose(
                        out=psf[:], in_=scores[:, kb * 128:(kb + 1) * 128], identity=idtf[:]))
                    p.v(lambda e: e.tensor_copy(out=pT[:], in_=psf[:]))
                    for dh in range(2):
                        MM(p, ctx[:, dh * 1024:(dh + 1) * 1024], pT[:],
                           vt2[:, g, dh * 1024:(dh + 1) * 1024],
                           start=(kb == 0), stop=(kb == 127), skip=True)
            p.v(lambda e: e.tensor_scalar(out=ctxb[:], in0=ctx[:], scalar1=st[:, 3:4],
                                          scalar2=None, op0=OP.mult))
            for j in range(16):
                p.t(lambda e, j=j: e.transpose(out=psb[:], in_=ctxb[:, j * 128:(j + 1) * 128],
                                               identity=idt[:]))
                p.v(lambda e, j=j: e.tensor_copy(out=cT[:, j, :], in_=psb[:]))
            for dh in range(2):
                for j in range(16):
                    MM(p, scp[:], cT[:, j, :], woT[:, j, dh * 1024:(dh + 1) * 1024],
                       start=(j == 0), stop=(j == 15))
                p.v(lambda e: e.tensor_copy(out=yf[:], in_=scp[:]))
                p.dma(lambda e, tt=tt, dh=dh: e.dma_start(
                    out=ao[tt * 128:(tt + 1) * 128, dh * 1024:(dh + 1) * 1024], in_=yf[:]))
        p.emit(nc, sem, block)
    return nc



def build_ffn0():
    nc = bass.Bass("TRN2", target_bir_lowering=False)
    xnt = nc.dram_tensor("xnt", [D, TPC], BF16, kind="ExternalInput").ap()
    wgT = nc.dram_tensor("wgT", [D, M0], BF16, kind="ExternalInput").ap()
    wuT = nc.dram_tensor("wuT", [D, M0], BF16, kind="ExternalInput").ap()
    wdT = nc.dram_tensor("wdT", [M0, D], BF16, kind="ExternalInput").ap()
    x = nc.dram_tensor("x", [TPC, D], F32, kind="ExternalInput").ap()
    at = nc.dram_tensor("at", [TPC, D], F32, kind="ExternalInput").ap()
    bob = nc.dram_tensor("bob", [128, D], F32, kind="ExternalInput").ap()
    x1 = nc.dram_tensor("x1", [TPC, D], F32, kind="ExternalOutput").ap()
    xn1 = nc.dram_tensor("xn1", [TPC, D], BF16, kind="ExternalOutput").ap()
    hT = nc.dram_tensor("hT", [M0, TPC], BF16, kind="Internal").ap()

    p = Prog()
    es = ExitStack()
    with es:
        xnc = es.enter_context(nc.sbuf_tensor("xnc", [128, 16, 1024], BF16))
        wgc = es.enter_context(nc.sbuf_tensor("wgc", [128, 16, 512], BF16))
        wuc = es.enter_context(nc.sbuf_tensor("wuc", [128, 16, 512], BF16))
        gg = es.enter_context(nc.sbuf_tensor("gg", [128, 1024], F32))
        sq = es.enter_context(nc.sbuf_tensor("sq", [128, 1024], F32))
        ua = es.enter_context(nc.sbuf_tensor("ua", [128, 1024], F32))
        hb = es.enter_context(nc.sbuf_tensor("hb", [128, 1024], BF16))
        wdc = es.enter_context(nc.sbuf_tensor("wdc", [128, 4, D], BF16))
        htc = es.enter_context(nc.sbuf_tensor("htc", [128, 4, 128], BF16))
        xtl = es.enter_context(nc.sbuf_tensor("xtl", [128, D], F32))
        atl = es.enter_context(nc.sbuf_tensor("atl", [128, D], F32))
        x1t = es.enter_context(nc.sbuf_tensor("x1t", [128, D], F32))
        xc = es.enter_context(nc.sbuf_tensor("xc", [128, D], F32))
        sq2 = es.enter_context(nc.sbuf_tensor("sq2", [128, D], F32))
        bo = es.enter_context(nc.sbuf_tensor("bo", [128, D], F32))
        xnb = es.enter_context(nc.sbuf_tensor("xnb", [128, D], BF16))
        st = es.enter_context(nc.sbuf_tensor("st", [128, 8], F32))
        gp = es.enter_context(nc.psum_tensor("gp", [128, 1024], F32))
        up = es.enter_context(nc.psum_tensor("up", [128, 1024], F32))
        y = es.enter_context(nc.psum_tensor("y", [128, D], F32))
        sem = es.enter_context(nc.semaphore("sem"))
        block = es.enter_context(nc.Block())
        p.dma(lambda e: e.dma_start(out=bo[:], in_=bob))
        p.v(lambda e: e.memset(st[:, 4:5], EPS))
        for tc in range(TPC // 1024):
            p.dma(lambda e, tc=tc: e.dma_start(
                out=xnc[:], in_=rearr_pj(xnt[:, tc * 1024:(tc + 1) * 1024])))
            for ic in range(M0 // 512):
                p.dma(lambda e, ic=ic: e.dma_start(
                    out=wgc[:], in_=rearr_pj(wgT[:, ic * 512:(ic + 1) * 512])))
                p.dma(lambda e, ic=ic: e.dma_start(
                    out=wuc[:], in_=rearr_pj(wuT[:, ic * 512:(ic + 1) * 512])))
                for sb in range(4):
                    for j in range(16):
                        MM(p, gp[:], wgc[:, j, sb * 128:(sb + 1) * 128],
                           xnc[:, j, :], start=(j == 0), stop=(j == 15))
                    for j in range(16):
                        MM(p, up[:], wuc[:, j, sb * 128:(sb + 1) * 128],
                           xnc[:, j, :], start=(j == 0), stop=(j == 15))
                    p.s(lambda e: e.activation(out=gg[:], in_=gp[:], func=AF.Copy))
                    p.s(lambda e: e.activation(out=sq[:], in_=gg[:], func=AF.Square))
                    p.v(lambda e: e.tensor_scalar(out=sq[:], in0=sq[:], scalar1=C1,
                                                  scalar2=C2, op0=OP.mult, op1=OP.add))
                    p.v(lambda e: e.tensor_tensor(out=ua[:], in0=gg[:], in1=sq[:], op=OP.mult))
                    p.s(lambda e: e.activation(out=ua[:], in_=ua[:], func=AF.Tanh))
                    p.v(lambda e: e.tensor_scalar(out=ua[:], in0=ua[:], scalar1=1.0,
                                                  scalar2=0.5, op0=OP.add, op1=OP.mult))
                    p.v(lambda e: e.tensor_tensor(out=ua[:], in0=ua[:], in1=gg[:], op=OP.mult))
                    p.v(lambda e: e.tensor_tensor(out=hb[:], in0=ua[:], in1=up[:], op=OP.mult))
                    p.dma(lambda e, ic=ic, sb=sb, tc=tc: e.dma_start(
                        out=hT[ic * 512 + sb * 128: ic * 512 + (sb + 1) * 128,
                               tc * 1024:(tc + 1) * 1024], in_=hb[:]))
        for tt in range(TPC // 128):
            t0 = tt * 128
            for ic in range(M0 // 512):
                p.dma(lambda e, ic=ic: e.dma_start(
                    out=wdc[:], in_=wdT[ic * 512:(ic + 1) * 512, :].rearrange(
                        "(s p) d -> p s d", p=128)))
                p.dma(lambda e, ic=ic, t0=t0: e.dma_start(
                    out=htc[:], in_=hT[ic * 512:(ic + 1) * 512, t0:t0 + 128].rearrange(
                        "(s p) t -> p s t", p=128)))
                for sb in range(4):
                    for dh in range(2):
                        MM(p, y[:, dh * 1024:(dh + 1) * 1024], htc[:, sb, :],
                           wdc[:, sb, dh * 1024:(dh + 1) * 1024],
                           start=(ic == 0 and sb == 0), stop=(ic == 31 and sb == 3))
            p.dma(lambda e, t0=t0: e.dma_start(out=xtl[:], in_=x[t0:t0 + 128, :]))
            p.dma(lambda e, t0=t0: e.dma_start(out=atl[:], in_=at[t0:t0 + 128, :]))
            p.v(lambda e: e.tensor_tensor(out=x1t[:], in0=y[:], in1=xtl[:], op=OP.add))
            p.v(lambda e: e.tensor_tensor(out=x1t[:], in0=x1t[:], in1=atl[:], op=OP.add))
            p.v(lambda e: e.tensor_tensor(out=x1t[:], in0=x1t[:], in1=bo[:], op=OP.add))
            p.dma(lambda e, t0=t0: e.dma_start(out=x1[t0:t0 + 128, :], in_=x1t[:]))
            p.v(lambda e: e.tensor_reduce(out=st[:, 0:1], in_=x1t[:], axis=AX.X, op=OP.add))
            p.v(lambda e: e.tensor_scalar(out=st[:, 0:1], in0=st[:, 0:1],
                                          scalar1=1.0 / D, scalar2=None, op0=OP.mult))
            p.v(lambda e: e.tensor_scalar(out=xc[:], in0=x1t[:], scalar1=st[:, 0:1],
                                          scalar2=None, op0=OP.subtract))
            p.s(lambda e: e.activation(out=sq2[:], in_=xc[:], func=AF.Square))
            p.v(lambda e: e.tensor_reduce(out=st[:, 1:2], in_=sq2[:], axis=AX.X, op=OP.add))
            p.s(lambda e: e.activation(out=st[:, 2:3], in_=st[:, 1:2], func=AF.Sqrt,
                                       scale=1.0 / D, bias=st[:, 4:5]))
            p.v(lambda e: e.reciprocal(out=st[:, 3:4], in_=st[:, 2:3]))
            p.v(lambda e: e.tensor_scalar(out=xnb[:], in0=xc[:], scalar1=st[:, 3:4],
                                          scalar2=None, op0=OP.mult))
            p.dma(lambda e, t0=t0: e.dma_start(out=xn1[t0:t0 + 128, :], in_=xnb[:]))
        p.emit(nc, sem, block)
    return nc



def build_mixprj():
    nc = bass.Bass("TRN2", target_bir_lowering=False)
    xh = nc.dram_tensor("xh", [D, TPC + 3], BF16, kind="ExternalInput").ap()
    wq = nc.dram_tensor("wq", [D, D], F32, kind="ExternalInput").ap()
    wk = nc.dram_tensor("wk", [D, D], F32, kind="ExternalInput").ap()
    wv = nc.dram_tensor("wv", [D, D], F32, kind="ExternalInput").ap()
    wig = nc.dram_tensor("wig", [D, D], F32, kind="ExternalInput").ap()
    wog = nc.dram_tensor("wog", [D, D], F32, kind="ExternalInput").ap()
    wgm = nc.dram_tensor("wgm", [NH, D], F32, kind="ExternalInput").ap()
    cw = nc.dram_tensor("cw", [D, 4], F32, kind="ExternalInput").ap()
    cb = nc.dram_tensor("cb", [D, 1], F32, kind="ExternalInput").ap()
    bigb = nc.dram_tensor("bigb", [128, D], F32, kind="ExternalInput").ap()
    bogb = nc.dram_tensor("bogb", [128, D], F32, kind="ExternalInput").ap()
    bgb = nc.dram_tensor("bgb", [128, NH], F32, kind="ExternalInput").ap()
    idn = nc.dram_tensor("idn", [128, 128], BF16, kind="ExternalInput").ap()
    idf = nc.dram_tensor("idf", [128, 128], F32, kind="ExternalInput").ap()
    qn = nc.dram_tensor("qn", [TPC, D], BF16, kind="ExternalOutput").ap()
    og = nc.dram_tensor("og", [TPC, D], BF16, kind="ExternalOutput").ap()
    btT = nc.dram_tensor("btT", [D, TPC], BF16, kind="ExternalOutput").ap()
    gmT = nc.dram_tensor("gmT", [NH, TPC], F32, kind="ExternalOutput").ap()
    xcT = nc.dram_tensor("xcT", [D, TPC], BF16, kind="Internal").ap()
    kns = nc.dram_tensor("kns", [TPC, D], BF16, kind="Internal").ap()
    vs = nc.dram_tensor("vs", [TPC, D], BF16, kind="Internal").ap()
    igs = nc.dram_tensor("igs", [TPC, D], BF16, kind="Internal").ap()

    p = Prog()
    es = ExitStack()
    with es:
        xr = es.enter_context(nc.sbuf_tensor("xr", [128, TPC + 3], BF16))
        acc = es.enter_context(nc.sbuf_tensor("acc", [128, TPC], F32))
        xcb = es.enter_context(nc.sbuf_tensor("xcb", [128, TPC], BF16))
        cwt = es.enter_context(nc.sbuf_tensor("cwt", [128, 4], F32))
        cbt = es.enter_context(nc.sbuf_tensor("cbt", [128, 1], F32))
        wtf = es.enter_context(nc.sbuf_tensor("wtf", [128, D], F32))
        wtb = es.enter_context(nc.sbuf_tensor("wtb", [128, D], BF16))
        wT = es.enter_context(nc.sbuf_tensor("wT", [128, 16, D], BF16))
        stile = es.enter_context(nc.sbuf_tensor("stile", [128, 16, 128], BF16))
        of = es.enter_context(nc.sbuf_tensor("of", [128, D], F32))
        ob = es.enter_context(nc.sbuf_tensor("ob", [128, D], BF16))
        sq = es.enter_context(nc.sbuf_tensor("sq", [128, D], F32))
        nr = es.enter_context(nc.sbuf_tensor("nr", [128, NH], F32))
        big = es.enter_context(nc.sbuf_tensor("big", [128, D], F32))
        bog = es.enter_context(nc.sbuf_tensor("bog", [128, D], F32))
        bgt = es.enter_context(nc.sbuf_tensor("bgt", [128, NH], F32))
        wgb = es.enter_context(nc.sbuf_tensor("wgb", [NH, D], BF16))
        wgf = es.enter_context(nc.sbuf_tensor("wgf", [NH, D], F32))
        wgT = es.enter_context(nc.sbuf_tensor("wgT", [128, 16, NH], BF16))
        gm = es.enter_context(nc.sbuf_tensor("gm", [128, NH], F32))
        k1 = es.enter_context(nc.sbuf_tensor("k1", [128, D], BF16))
        v1 = es.enter_context(nc.sbuf_tensor("v1", [128, D], BF16))
        i1 = es.enter_context(nc.sbuf_tensor("i1", [128, D], BF16))
        btf = es.enter_context(nc.sbuf_tensor("btf", [128, D], F32))
        btb = es.enter_context(nc.sbuf_tensor("btb", [128, D], BF16))
        idt = es.enter_context(nc.sbuf_tensor("idt", [128, 128], BF16))
        idtf = es.enter_context(nc.sbuf_tensor("idtf", [128, 128], F32))
        psb = es.enter_context(nc.psum_tensor("psb", [128, 128], BF16))
        psf = es.enter_context(nc.psum_tensor("psf", [128, 128], F32))
        tb = es.enter_context(nc.sbuf_tensor("tb", [128, 128], BF16))
        tf = es.enter_context(nc.sbuf_tensor("tf", [128, 128], F32))
        mm = es.enter_context(nc.psum_tensor("mm", [128, 1024], F32))
        mg = es.enter_context(nc.psum_tensor("mg", [128, NH], F32))
        sem = es.enter_context(nc.semaphore("sem"))
        block = es.enter_context(nc.Block())
        p.dma(lambda e: e.dma_start(out=idt[:], in_=idn))
        p.dma(lambda e: e.dma_start(out=idtf[:], in_=idf))
        p.dma(lambda e: e.dma_start(out=big[:], in_=bigb))
        p.dma(lambda e: e.dma_start(out=bog[:], in_=bogb))
        p.dma(lambda e: e.dma_start(out=bgt[:], in_=bgb))
        for j in range(16):
            p.dma(lambda e, j=j: e.dma_start(out=xr[:], in_=xh[j * 128:(j + 1) * 128, :]))
            p.dma(lambda e, j=j: e.dma_start(out=cwt[:], in_=cw[j * 128:(j + 1) * 128, :]))
            p.dma(lambda e, j=j: e.dma_start(out=cbt[:], in_=cb[j * 128:(j + 1) * 128, :]))
            p.v(lambda e: e.tensor_scalar(out=acc[:], in0=xr[:, 0:TPC],
                                          scalar1=cwt[:, 0:1], scalar2=None, op0=OP.mult))
            for dj in range(1, 4):
                p.v(lambda e, dj=dj: e.scalar_tensor_tensor(
                    out=acc[:], in0=xr[:, dj:dj + TPC], scalar=cwt[:, dj:dj + 1],
                    in1=acc[:], op0=OP.mult, op1=OP.add))
            p.s(lambda e: e.activation(out=xcb[:], in_=acc[:], func=AF.Silu, bias=cbt[:, 0:1]))
            p.dma(lambda e, j=j: e.dma_start(out=xcT[j * 128:(j + 1) * 128, :], in_=xcb[:]))
        def proj(wsrc, src, handler):
            for r in range(16):
                p.dma(lambda e, r=r, wsrc=wsrc: e.dma_start(
                    out=wtf[:], in_=wsrc[r * 128:(r + 1) * 128, :]))
                p.s(lambda e: e.activation(out=wtb[:], in_=wtf[:], func=AF.Copy))
                for j in range(16):
                    p.t(lambda e, j=j: e.transpose(
                        out=psb[:], in_=wtb[:, j * 128:(j + 1) * 128], identity=idt[:]))
                    p.v(lambda e, j=j, r=r: e.tensor_copy(
                        out=wT[:, j, r * 128:(r + 1) * 128], in_=psb[:]))
            for tt in range(TPC // 128):
                p.dma(lambda e, tt=tt, src=src: e.dma_start(
                    out=stile[:], in_=rearr_pj(src[:, tt * 128:(tt + 1) * 128])))
                for dh in range(2):
                    for j in range(16):
                        MM(p, mm[:], stile[:, j, :],
                           wT[:, j, dh * 1024:(dh + 1) * 1024],
                           start=(j == 0), stop=(j == 15))
                    p.s(lambda e, dh=dh: e.activation(
                        out=of[:, dh * 1024:(dh + 1) * 1024], in_=mm[:], func=AF.Copy))
                handler(tt)

        def l2n_handler(dst):
            def h(tt):
                p.s(lambda e: e.activation(out=sq[:], in_=of[:], func=AF.Square))
                for hh in range(NH):
                    p.v(lambda e, hh=hh: e.tensor_reduce(
                        out=nr[:, hh:hh + 1], in_=sq[:, hh * 128:(hh + 1) * 128],
                        axis=AX.X, op=OP.add))
                p.s(lambda e: e.activation(out=nr[:], in_=nr[:], func=AF.Sqrt))
                p.v(lambda e: e.tensor_scalar(out=nr[:], in0=nr[:], scalar1=1e-12,
                                              scalar2=None, op0=OP.max))
                p.v(lambda e: e.reciprocal(out=nr[:], in_=nr[:]))
                p.v(lambda e: e.tensor_tensor(
                    out=ob.rearrange("p (h d) -> p h d", h=NH)[:],
                    in0=of.rearrange("p (h d) -> p h d", h=NH)[:],
                    in1=nr[:].unsqueeze(2).broadcast_to((128, NH, 128)), op=OP.mult))
                p.dma(lambda e, tt=tt, dst=dst: e.dma_start(
                    out=dst[tt * 128:(tt + 1) * 128, :], in_=ob[:]))
            return h

        def plain_handler(dst):
            def h(tt):
                p.v(lambda e: e.tensor_copy(out=ob[:], in_=of[:]))
                p.dma(lambda e, tt=tt, dst=dst: e.dma_start(
                    out=dst[tt * 128:(tt + 1) * 128, :], in_=ob[:]))
            return h

        def sig_handler(dst, bias):
            def h(tt):
                p.v(lambda e, bias=bias: e.tensor_tensor(out=of[:], in0=of[:], in1=bias[:],
                                                         op=OP.add))
                p.s(lambda e: e.activation(out=ob[:], in_=of[:], func=AF.Sigmoid))
                p.dma(lambda e, tt=tt, dst=dst: e.dma_start(
                    out=dst[tt * 128:(tt + 1) * 128, :], in_=ob[:]))
            return h

        proj(wq, xh[:, 3:], l2n_handler(qn))
        proj(wk, xh[:, 3:], l2n_handler(kns))
        proj(wv, xh[:, 3:], plain_handler(vs))
        proj(wig, xcT, sig_handler(igs, big))
        proj(wog, xcT, sig_handler(og, bog))
        p.dma(lambda e: e.dma_start(out=wgf[:], in_=wgm))
        p.s(lambda e: e.activation(out=wgb[:], in_=wgf[:], func=AF.Copy))
        for j in range(16):
            p.t(lambda e, j=j: e.transpose(out=psb[0:128, 0:NH],
                                           in_=wgb[:, j * 128:(j + 1) * 128],
                                           identity=idt[0:NH, 0:NH]))
            p.v(lambda e, j=j: e.tensor_copy(out=wgT[:, j, :], in_=psb[0:128, 0:NH]))
        for tt in range(TPC // 128):
            p.dma(lambda e, tt=tt: e.dma_start(
                out=stile[:], in_=rearr_pj(xcT[:, tt * 128:(tt + 1) * 128])))
            for j in range(16):
                p.t(lambda e, j=j: e.matmul(out=mg[:], lhsT=stile[:, j, :], rhs=wgT[:, j, :],
                                            start=(j == 0), stop=(j == 15)))
            p.v(lambda e: e.tensor_tensor(out=gm[:], in0=mg[:], in1=bgt[:], op=OP.add))
            p.s(lambda e: e.activation(out=gm[:], in_=gm[:], func=AF.Sigmoid))
            p.t(lambda e: e.transpose(out=psf[0:NH, 0:128], in_=gm[:], identity=idtf[:]))
            p.v(lambda e: e.tensor_copy(out=tf[0:NH, 0:128], in_=psf[0:NH, 0:128]))
            p.dma(lambda e, tt=tt: e.dma_start(
                out=gmT[:, tt * 128:(tt + 1) * 128], in_=tf[0:NH, 0:128]))
            p.dma(lambda e, tt=tt: e.dma_start(out=k1[:], in_=kns[tt * 128:(tt + 1) * 128, :]))
            p.dma(lambda e, tt=tt: e.dma_start(out=v1[:], in_=vs[tt * 128:(tt + 1) * 128, :]))
            p.dma(lambda e, tt=tt: e.dma_start(out=i1[:], in_=igs[tt * 128:(tt + 1) * 128, :]))
            p.v(lambda e: e.tensor_tensor(out=btf[:], in0=i1[:], in1=k1[:], op=OP.mult))
            p.v(lambda e: e.tensor_tensor(out=btb[:], in0=btf[:], in1=v1[:], op=OP.mult))
            for j in range(16):
                p.t(lambda e, j=j: e.transpose(out=psb[:], in_=btb[:, j * 128:(j + 1) * 128],
                                               identity=idt[:]))
                p.v(lambda e: e.tensor_copy(out=tb[:], in_=psb[:]))
                p.dma(lambda e, j=j, tt=tt: e.dma_start(
                    out=btT[j * 128:(j + 1) * 128, tt * 128:(tt + 1) * 128], in_=tb[:]))
        p.emit(nc, sem, block)
    return nc



def build_scan():
    nc = bass.Bass("TRN2", target_bir_lowering=False)
    bt = nc.dram_tensor("bt", [512, S], BF16, kind="ExternalInput").ap()
    gmr = nc.dram_tensor("gmr", [512, S], F32, kind="ExternalInput").ap()
    idn = nc.dram_tensor("idn", [128, 128], BF16, kind="ExternalInput").ap()
    mo = nc.dram_tensor("mo", [S, 512], BF16, kind="ExternalOutput").ap()

    p = Prog()
    es = ExitStack()
    with es:
        btc = es.enter_context(nc.sbuf_tensor("btc", [128, 2048], BF16))
        gmc = es.enter_context(nc.sbuf_tensor("gmc", [128, 2048], F32))
        mmb = es.enter_context(nc.sbuf_tensor("mm", [128, 2048], F32))
        mb = es.enter_context(nc.sbuf_tensor("mb", [128, 2048], BF16))
        carry = es.enter_context(nc.sbuf_tensor("carry", [128, 1], F32))
        idt = es.enter_context(nc.sbuf_tensor("idt", [128, 128], BF16))
        psb = es.enter_context(nc.psum_tensor("psb", [128, 128], BF16))
        tb = es.enter_context(nc.sbuf_tensor("tb", [128, 128], BF16))
        sem = es.enter_context(nc.semaphore("sem"))
        block = es.enter_context(nc.Block())
        p.dma(lambda e: e.dma_start(out=idt[:], in_=idn))
        for jt in range(4):
            for ch in range(S // 2048):
                p.dma(lambda e, jt=jt, ch=ch: e.dma_start(
                    out=btc[:], in_=bt[jt * 128:(jt + 1) * 128, ch * 2048:(ch + 1) * 2048]))
                p.dma(lambda e, jt=jt, ch=ch: e.dma_start(
                    out=gmc[:], in_=gmr[jt * 128:(jt + 1) * 128, ch * 2048:(ch + 1) * 2048]))
                for sg in range(4):
                    if ch == 0 and sg == 0:
                        p.v(lambda e: e.tensor_tensor_scan(
                            out=mmb[:, 0:512], data0=gmc[:, 0:512], data1=btc[:, 0:512],
                            initial=0.0, op0=OP.mult, op1=OP.add))
                    else:
                        p.v(lambda e, sg=sg: e.tensor_tensor_scan(
                            out=mmb[:, sg * 512:(sg + 1) * 512],
                            data0=gmc[:, sg * 512:(sg + 1) * 512],
                            data1=btc[:, sg * 512:(sg + 1) * 512],
                            initial=carry[:, 0:1], op0=OP.mult, op1=OP.add))
                    p.v(lambda e, sg=sg: e.tensor_copy(
                        out=carry[:, 0:1], in_=mmb[:, sg * 512 + 511:sg * 512 + 512]))
                p.v(lambda e: e.tensor_copy(out=mb[:], in_=mmb[:]))
                for bb in range(16):
                    p.t(lambda e, bb=bb: e.transpose(
                        out=psb[:], in_=mb[:, bb * 128:(bb + 1) * 128], identity=idt[:]))
                    p.v(lambda e: e.tensor_copy(out=tb[:], in_=psb[:]))
                    p.dma(lambda e, bb=bb, jt=jt, ch=ch: e.dma_start(
                        out=mo[ch * 2048 + bb * 128: ch * 2048 + (bb + 1) * 128,
                               jt * 128:(jt + 1) * 128], in_=tb[:]))
        p.emit(nc, sem, block)
    return nc



def build_final():
    nc = bass.Bass("TRN2", target_bir_lowering=False)
    mem = nc.dram_tensor("mem", [TPC, D], BF16, kind="ExternalInput").ap()
    qn = nc.dram_tensor("qn", [TPC, D], BF16, kind="ExternalInput").ap()
    og = nc.dram_tensor("og", [TPC, D], BF16, kind="ExternalInput").ap()
    x1 = nc.dram_tensor("x1", [TPC, D], F32, kind="ExternalInput").ap()
    xn1t = nc.dram_tensor("xn1t", [D, TPC], BF16, kind="ExternalInput").ap()
    wo = nc.dram_tensor("wo", [D, D], F32, kind="ExternalInput").ap()
    mnw = nc.dram_tensor("mnw", [128, D], F32, kind="ExternalInput").ap()
    mnb = nc.dram_tensor("mnb", [128, D], F32, kind="ExternalInput").ap()
    gnw = nc.dram_tensor("gnw", [128, D], F32, kind="ExternalInput").ap()
    gnb = nc.dram_tensor("gnb", [128, D], F32, kind="ExternalInput").ap()
    ggT = nc.dram_tensor("ggT", [D, IN1], BF16, kind="ExternalInput").ap()
    guT = nc.dram_tensor("guT", [D, IN1], BF16, kind="ExternalInput").ap()
    gdT = nc.dram_tensor("gdT", [IN1, D], BF16, kind="ExternalInput").ap()
    idn = nc.dram_tensor("idn", [128, 128], BF16, kind="ExternalInput").ap()
    out = nc.dram_tensor("out", [TPC, D], F32, kind="ExternalOutput").ap()
    hT = nc.dram_tensor("hT", [IN1, TPC], BF16, kind="Internal").ap()

    NB1 = IN1 // 128

    p = Prog()
    es = ExitStack()
    with es:
        woT = es.enter_context(nc.sbuf_tensor("woT", [128, 16, D], BF16))
        wtf = es.enter_context(nc.sbuf_tensor("wtf", [128, D], F32))
        wtb = es.enter_context(nc.sbuf_tensor("wtb", [128, D], BF16))
        gs = es.enter_context(nc.sbuf_tensor("gs", [128, 1024], F32))
        hb = es.enter_context(nc.sbuf_tensor("hb", [128, 1024], BF16))
        mnwt = es.enter_context(nc.sbuf_tensor("mnwt", [128, D], F32))
        mnbt = es.enter_context(nc.sbuf_tensor("mnbt", [128, D], F32))
        gnwt = es.enter_context(nc.sbuf_tensor("gnwt", [128, D], F32))
        gnbt = es.enter_context(nc.sbuf_tensor("gnbt", [128, D], F32))
        mt = es.enter_context(nc.sbuf_tensor("mt", [128, D], BF16))
        qt = es.enter_context(nc.sbuf_tensor("qt", [128, D], BF16))
        ot = es.enter_context(nc.sbuf_tensor("ot", [128, D], BF16))
        x1t = es.enter_context(nc.sbuf_tensor("x1t", [128, D], F32))
        o1 = es.enter_context(nc.sbuf_tensor("o1", [128, D], F32))
        o2 = es.enter_context(nc.sbuf_tensor("o2", [128, D], F32))
        sq = es.enter_context(nc.sbuf_tensor("sq", [128, D], F32))
        mu = es.enter_context(nc.sbuf_tensor("mu", [128, NH], F32))
        s2 = es.enter_context(nc.sbuf_tensor("s2", [128, NH], F32))
        rs = es.enter_context(nc.sbuf_tensor("rs", [128, NH], F32))
        ept = es.enter_context(nc.sbuf_tensor("ept", [128, 1], F32))
        obf = es.enter_context(nc.sbuf_tensor("obf", [128, D], BF16))
        cT = es.enter_context(nc.sbuf_tensor("cT", [128, 16, 128], BF16))
        yb = es.enter_context(nc.sbuf_tensor("yb", [128, D], F32))
        yb2 = es.enter_context(nc.sbuf_tensor("yb2", [128, D], F32))
        htb = es.enter_context(nc.sbuf_tensor("htb", [128, 128], BF16))
        gdc = es.enter_context(nc.sbuf_tensor("gdc", [128, 1024], BF16))
        idt = es.enter_context(nc.sbuf_tensor("idt", [128, 128], BF16))
        psb = es.enter_context(nc.psum_tensor("psb", [128, 128], BF16))
        gp = es.enter_context(nc.psum_tensor("gp", [128, 1024], F32))
        up = es.enter_context(nc.psum_tensor("up", [128, 1024], F32))
        y2 = es.enter_context(nc.psum_tensor("y2", [128, 1024], F32))
        sem = es.enter_context(nc.semaphore("sem"))
        block = es.enter_context(nc.Block())
        ggc = woT[:, :, 0:512]
        guc = woT[:, :, 512:1024]
        xnc = woT[:, :, 1024:2048]
        p.dma(lambda e: e.dma_start(out=idt[:], in_=idn))
        p.v(lambda e: e.memset(ept[:], EPS))
        for tc in range(TPC // 1024):
            p.dma(lambda e, tc=tc: e.dma_start(
                out=xnc, in_=rearr_pj(xn1t[:, tc * 1024:(tc + 1) * 1024])))
            for ic in range((IN1 + 511) // 512):
                i0 = ic * 512
                cwd = min(512, IN1 - i0)
                p.dma(lambda e, i0=i0, cwd=cwd: e.dma_start(
                    out=ggc[:, :, 0:cwd], in_=rearr_pj(ggT[:, i0:i0 + cwd])))
                p.dma(lambda e, i0=i0, cwd=cwd: e.dma_start(
                    out=guc[:, :, 0:cwd], in_=rearr_pj(guT[:, i0:i0 + cwd])))
                for sb in range(cwd // 128):
                    for j in range(16):
                        MM(p, gp[:], ggc[:, j, sb * 128:(sb + 1) * 128],
                           xnc[:, j, :], start=(j == 0), stop=(j == 15))
                    for j in range(16):
                        MM(p, up[:], guc[:, j, sb * 128:(sb + 1) * 128],
                           xnc[:, j, :], start=(j == 0), stop=(j == 15))
                    p.s(lambda e: e.activation(out=gs[:], in_=gp[:], func=AF.Silu))
                    p.v(lambda e: e.tensor_tensor(out=hb[:], in0=gs[:], in1=up[:], op=OP.mult))
                    p.dma(lambda e, i0=i0, sb=sb, tc=tc: e.dma_start(
                        out=hT[i0 + sb * 128: i0 + (sb + 1) * 128,
                               tc * 1024:(tc + 1) * 1024], in_=hb[:]))
        for r in range(16):
            p.dma(lambda e, r=r: e.dma_start(out=wtf[:], in_=wo[r * 128:(r + 1) * 128, :]))
            p.s(lambda e: e.activation(out=wtb[:], in_=wtf[:], func=AF.Copy))
            for j in range(16):
                p.t(lambda e, j=j: e.transpose(out=psb[:], in_=wtb[:, j * 128:(j + 1) * 128],
                                               identity=idt[:]))
                p.v(lambda e, j=j, r=r: e.tensor_copy(out=woT[:, j, r * 128:(r + 1) * 128],
                                                      in_=psb[:]))
        p.dma(lambda e: e.dma_start(out=mnwt[:], in_=mnw))
        p.dma(lambda e: e.dma_start(out=mnbt[:], in_=mnb))
        p.dma(lambda e: e.dma_start(out=gnwt[:], in_=gnw))
        p.dma(lambda e: e.dma_start(out=gnbt[:], in_=gnb))

        def gnorm(srcv, wt, bt_, dst):
            for hh in range(NH):
                p.v(lambda e, hh=hh: e.tensor_reduce(
                    out=mu[:, hh:hh + 1], in_=srcv[:, hh * 128:(hh + 1) * 128],
                    axis=AX.X, op=OP.add))
            p.v(lambda e: e.tensor_scalar(out=mu[:], in0=mu[:], scalar1=1.0 / 128,
                                          scalar2=None, op0=OP.mult))
            p.s(lambda e: e.activation(out=sq[:], in_=srcv[:], func=AF.Square))
            for hh in range(NH):
                p.v(lambda e, hh=hh: e.tensor_reduce(
                    out=s2[:, hh:hh + 1], in_=sq[:, hh * 128:(hh + 1) * 128],
                    axis=AX.X, op=OP.add))
            p.v(lambda e: e.tensor_scalar(out=s2[:], in0=s2[:], scalar1=1.0 / 128,
                                          scalar2=None, op0=OP.mult))
            p.s(lambda e: e.activation(out=rs[:], in_=mu[:], func=AF.Square))
            p.v(lambda e: e.tensor_tensor(out=s2[:], in0=s2[:], in1=rs[:], op=OP.subtract))
            p.s(lambda e: e.activation(out=s2[:], in_=s2[:], func=AF.Sqrt, bias=ept[:, 0:1]))
            p.v(lambda e: e.reciprocal(out=rs[:], in_=s2[:]))
            sv = srcv.rearrange("p (h d) -> p h d", h=NH)
            dv = dst.rearrange("p (h d) -> p h d", h=NH)
            p.v(lambda e: e.tensor_tensor(
                out=dv[:], in0=sv[:], in1=mu[:].unsqueeze(2).broadcast_to((128, NH, 128)),
                op=OP.subtract))
            p.v(lambda e: e.tensor_tensor(
                out=dv[:], in0=dv[:], in1=rs[:].unsqueeze(2).broadcast_to((128, NH, 128)),
                op=OP.mult))
            p.v(lambda e, wt=wt: e.tensor_tensor(out=dst[:], in0=dst[:], in1=wt[:], op=OP.mult))
            p.v(lambda e, bt_=bt_: e.tensor_tensor(out=dst[:], in0=dst[:], in1=bt_[:], op=OP.add))

        for tt in range(TPC // 128):
            t0 = tt * 128
            p.dma(lambda e, t0=t0: e.dma_start(out=mt[:], in_=mem[t0:t0 + 128, :]))
            p.dma(lambda e, t0=t0: e.dma_start(out=qt[:], in_=qn[t0:t0 + 128, :]))
            p.dma(lambda e, t0=t0: e.dma_start(out=ot[:], in_=og[t0:t0 + 128, :]))
            p.dma(lambda e, t0=t0: e.dma_start(out=x1t[:], in_=x1[t0:t0 + 128, :]))
            p.v(lambda e: e.tensor_copy(out=o1[:], in_=mt[:]))
            gnorm(o1[:], mnwt, mnbt, o2[:])
            p.v(lambda e: e.tensor_tensor(out=o1[:], in0=o2[:], in1=qt[:], op=OP.mult))
            gnorm(o1[:], gnwt, gnbt, o2[:])
            p.v(lambda e: e.tensor_tensor(out=obf[:], in0=o2[:], in1=ot[:], op=OP.mult))
            for j in range(16):
                p.t(lambda e, j=j: e.transpose(out=psb[:], in_=obf[:, j * 128:(j + 1) * 128],
                                               identity=idt[:]))
                p.v(lambda e, j=j: e.tensor_copy(out=cT[:, j, :], in_=psb[:]))
            for dh in range(2):
                for j in range(16):
                    MM(p, gp[:], cT[:, j, :], woT[:, j, dh * 1024:(dh + 1) * 1024],
                       start=(j == 0), stop=(j == 15))
                p.v(lambda e, dh=dh: e.tensor_copy(
                    out=yb[:, dh * 1024:(dh + 1) * 1024], in_=gp[:]))
            for dh in range(2):
                for ib in range(NB1):
                    p.dma(lambda e, ib=ib, t0=t0: e.dma_start(
                        out=htb[:], in_=hT[ib * 128:(ib + 1) * 128, t0:t0 + 128]))
                    p.dma(lambda e, ib=ib, dh=dh: e.dma_start(
                        out=gdc[:], in_=gdT[ib * 128:(ib + 1) * 128,
                                            dh * 1024:(dh + 1) * 1024]))
                    MM(p, y2[:], htb[:], gdc[:],
                       start=(ib == 0), stop=(ib == NB1 - 1))
                p.v(lambda e, dh=dh: e.tensor_copy(
                    out=yb2[:, dh * 1024:(dh + 1) * 1024], in_=y2[:]))
            p.v(lambda e: e.tensor_tensor(out=o1[:], in0=yb[:], in1=yb2[:], op=OP.add))
            p.v(lambda e: e.tensor_tensor(out=o1[:], in0=o1[:], in1=x1t[:], op=OP.add))
            p.dma(lambda e, t0=t0: e.dma_start(out=out[t0:t0 + 128, :], in_=o1[:]))
        p.emit(nc, sem, block)
    return nc



_cache = {}


def _get(name, builder):
    if name not in _cache:
        _cache[name] = builder()
    return _cache[name]


def kernel(**inp):
    f32 = np.float32
    idn_bf = np.ascontiguousarray(np.eye(128, dtype=f32).astype(BF))
    idn_f = np.ascontiguousarray(np.eye(128, dtype=f32))
    cores = list(range(NC))

    def run(nc, maps):
        return run_bass_kernel_spmd(nc, maps, core_ids=cores).results

    lut_bc = np.ascontiguousarray(np.broadcast_to(
        np.asarray(inp["L1_ffn_lut"], f32).reshape(1, 16), (128, 16)))
    maps = []
    for c in cores:
        maps.append({
            "gW": np.ascontiguousarray(inp["L0_ffn_gate_W_packed"][c * 2048:(c + 1) * 2048]),
            "gS": np.ascontiguousarray(inp["L0_ffn_gate_scales"][c * 2048:(c + 1) * 2048]),
            "uW": np.ascontiguousarray(inp["L0_ffn_up_W_packed"][c * 2048:(c + 1) * 2048]),
            "uS": np.ascontiguousarray(inp["L0_ffn_up_scales"][c * 2048:(c + 1) * 2048]),
            "dW": np.ascontiguousarray(inp["L0_ffn_down_W_packed"][:, c * 1024:(c + 1) * 1024]),
            "dS": np.ascontiguousarray(inp["L0_ffn_down_scales"][:, c * 64:(c + 1) * 64]),
            "ggI": np.ascontiguousarray(inp["L1_ffn_gate_idx"][c * 688:(c + 1) * 688]),
            "guI": np.ascontiguousarray(inp["L1_ffn_up_idx"][c * 688:(c + 1) * 688]),
            "gdI": np.ascontiguousarray(inp["L1_ffn_down_idx"][:, c * 688:(c + 1) * 688]),
            "lutb": lut_bc, "idn": idn_bf,
        })
    r1 = run(_get("dequant", build_dequant), maps)
    WgT = np.concatenate([r["wgt"] for r in r1], axis=1)
    WuT = np.concatenate([r["wut"] for r in r1], axis=1)
    WdT = np.concatenate([r["wdt"] for r in r1], axis=0)
    GgT = np.concatenate([r["ggt"] for r in r1], axis=1)
    GuT = np.concatenate([r["gut"] for r in r1], axis=1)
    GdT = np.concatenate([r["gdt"] for r in r1], axis=0)

    x_full = np.asarray(inp["L0_attn_hidden_states"], f32).reshape(T, D)
    wq_full = np.ascontiguousarray(np.asarray(inp["L0_attn_w_qkv"], f32))
    bq_full = np.ascontiguousarray(np.asarray(inp["L0_attn_b_qkv"], f32).reshape(3 * D, 1))
    maps = [{"x": np.ascontiguousarray(x_full[c * TPC:(c + 1) * TPC]),
             "wq": wq_full, "bq": bq_full, "idn": idn_bf} for c in cores]
    r2 = run(_get("qkv", build_qkv), maps)
    xnt_full = np.concatenate([r["xnt"] for r in r2], axis=1)
    qt_full = np.concatenate([r["qt"] for r in r2], axis=1)
    kt_full = np.concatenate([r["kt"] for r in r2], axis=1)
    v_full = np.concatenate([r["vo"] for r in r2], axis=0)

    wo0 = np.ascontiguousarray(np.asarray(inp["L0_attn_w_out"], f32))
    maps = []
    for c in cores:
        b = c // 4
        maps.append({
            "qts": np.ascontiguousarray(qt_full[:, c * TPC:(c + 1) * TPC]),
            "ktb": np.ascontiguousarray(kt_full[:, b * S:(b + 1) * S]),
            "vb": np.ascontiguousarray(v_full[b * S:(b + 1) * S]),
            "wo": wo0, "idn": idn_bf, "idf": idn_f,
        })
    r3 = run(_get("attn0", build_attn0), maps)
    attn_full = np.concatenate([r["ao"] for r in r3], axis=0)

    bo_bc = np.ascontiguousarray(np.broadcast_to(
        np.asarray(inp["L0_attn_b_out"], f32).reshape(1, D), (128, D)))
    maps = []
    for c in cores:
        maps.append({
            "xnt": np.ascontiguousarray(xnt_full[:, c * TPC:(c + 1) * TPC]),
            "wgT": np.ascontiguousarray(WgT), "wuT": np.ascontiguousarray(WuT),
            "wdT": np.ascontiguousarray(WdT),
            "x": np.ascontiguousarray(x_full[c * TPC:(c + 1) * TPC]),
            "at": np.ascontiguousarray(attn_full[c * TPC:(c + 1) * TPC]),
            "bob": bo_bc,
        })
    r4 = run(_get("ffn0", build_ffn0), maps)
    x1_full = np.concatenate([r["x1"] for r in r4], axis=0)
    xn1_full = np.concatenate([r["xn1"] for r in r4], axis=0)
    xn1t_full = np.ascontiguousarray(xn1_full.T)

    bz = np.zeros((D, 3), dtype=BF)
    big_bc = np.ascontiguousarray(np.broadcast_to(
        np.asarray(inp["L1_attn_b_ig"], f32).reshape(1, D), (128, D)))
    bog_bc = np.ascontiguousarray(np.broadcast_to(
        np.asarray(inp["L1_attn_b_og"], f32).reshape(1, D), (128, D)))
    bg_bc = np.ascontiguousarray(np.broadcast_to(
        np.asarray(inp["L1_attn_b_g"], f32).reshape(1, NH), (128, NH)))
    maps = []
    for c in cores:
        in_batch = c % 4
        if in_batch == 0:
            halo = bz
        else:
            halo = np.ascontiguousarray(xn1t_full[:, c * TPC - 3:c * TPC])
        xh = np.ascontiguousarray(np.concatenate(
            [halo, xn1t_full[:, c * TPC:(c + 1) * TPC]], axis=1))
        maps.append({
            "xh": xh,
            "wq": np.ascontiguousarray(np.asarray(inp["L1_attn_Wq"], f32)),
            "wk": np.ascontiguousarray(np.asarray(inp["L1_attn_Wk"], f32)),
            "wv": np.ascontiguousarray(np.asarray(inp["L1_attn_Wv"], f32)),
            "wig": np.ascontiguousarray(np.asarray(inp["L1_attn_Wig"], f32)),
            "wog": np.ascontiguousarray(np.asarray(inp["L1_attn_Wog"], f32)),
            "wgm": np.ascontiguousarray(np.asarray(inp["L1_attn_Wg"], f32)),
            "cw": np.ascontiguousarray(np.asarray(inp["L1_attn_conv_w"], f32)),
            "cb": np.ascontiguousarray(np.asarray(inp["L1_attn_conv_b"], f32).reshape(D, 1)),
            "bigb": big_bc, "bogb": bog_bc, "bgb": bg_bc,
            "idn": idn_bf, "idf": idn_f,
        })
    r5 = run(_get("mixprj", build_mixprj), maps)
    qn_full = np.concatenate([r["qn"] for r in r5], axis=0)
    og_full = np.concatenate([r["og"] for r in r5], axis=0)
    btT_b = [np.concatenate([r5[b * 4 + i]["btT"] for i in range(4)], axis=1)
             for b in range(B)]
    gmT_b = [np.concatenate([r5[b * 4 + i]["gmT"] for i in range(4)], axis=1)
             for b in range(B)]

    maps = []
    for c in cores:
        b = c // 4
        r0 = (c % 4) * 512
        gm_rows = np.repeat(gmT_b[b], 128, axis=0)[r0:r0 + 512]
        maps.append({
            "bt": np.ascontiguousarray(btT_b[b][r0:r0 + 512]),
            "gmr": np.ascontiguousarray(gm_rows),
            "idn": idn_bf,
        })
    r6 = run(_get("scan", build_scan), maps)
    mem_b = [np.concatenate([r6[b * 4 + i]["mo"] for i in range(4)], axis=1)
             for b in range(B)]
    mem_full = np.concatenate(mem_b, axis=0)

    mnw_bc = np.ascontiguousarray(np.broadcast_to(
        np.asarray(inp["L1_attn_mn_w"], f32).reshape(1, D), (128, D)))
    mnb_bc = np.ascontiguousarray(np.broadcast_to(
        np.asarray(inp["L1_attn_mn_b"], f32).reshape(1, D), (128, D)))
    gnw_bc = np.ascontiguousarray(np.broadcast_to(
        np.asarray(inp["L1_attn_gn_w"], f32).reshape(1, D), (128, D)))
    gnb_bc = np.ascontiguousarray(np.broadcast_to(
        np.asarray(inp["L1_attn_gn_b"], f32).reshape(1, D), (128, D)))
    wo1 = np.ascontiguousarray(np.asarray(inp["L1_attn_Wo"], f32))
    maps = []
    for c in cores:
        maps.append({
            "mem": np.ascontiguousarray(mem_full[c * TPC:(c + 1) * TPC]),
            "qn": np.ascontiguousarray(qn_full[c * TPC:(c + 1) * TPC]),
            "og": np.ascontiguousarray(og_full[c * TPC:(c + 1) * TPC]),
            "x1": np.ascontiguousarray(x1_full[c * TPC:(c + 1) * TPC]),
            "xn1t": np.ascontiguousarray(xn1t_full[:, c * TPC:(c + 1) * TPC]),
            "wo": wo1, "mnw": mnw_bc, "mnb": mnb_bc, "gnw": gnw_bc, "gnb": gnb_bc,
            "ggT": np.ascontiguousarray(GgT), "guT": np.ascontiguousarray(GuT),
            "gdT": np.ascontiguousarray(GdT), "idn": idn_bf,
        })
    r7 = run(_get("final", build_final), maps)
    out = np.concatenate([r["out"] for r in r7], axis=0).reshape(B, S, D)
    return out.astype(np.float32)
